# version 16
# speedup vs baseline: 1.1243x; 1.0250x over previous
import numpy as np
from contextlib import ExitStack
import concourse.bass as bass
import concourse.mybir as mybir
from concourse.bass_utils import run_bass_kernel_spmd

F32 = mybir.dt.float32
BF16 = mybir.dt.bfloat16
I32 = mybir.dt.int32
I8 = mybir.dt.int8
AF = mybir.ActivationFunctionType
ALU = mybir.AluOpType
AX = mybir.AxisListType

T, TO, H, NH, HD, FC = 4096, 2048, 1024, 16, 64, 8
TT = 512
NTT, NTTO = T // TT, TO // TT
NCH, NCHO = T // 128, TO // 128
EPS = 1e-5
PI32 = float(np.float32(np.pi))

ENG = ("te", "act", "dve", "pool", "sp")


class Em:
    def __init__(self, nc, es):
        self.nc = nc
        self.es = es
        self.streams = {e: [] for e in ENG}
        self.sem_n = {}
        self.sem_h = {}
        self.last_wait = {e: {} for e in ENG}

    def sem(self, name):
        self.sem_h[name] = self.es.enter_context(self.nc.semaphore(name))
        self.sem_n[name] = 0

    def on(self, eng, fn, waits=(), inc=None, inc_by=1, tag="", ss=False):
        waits = list(waits)
        if ss:
            waits.append((eng, self.sem_n[eng]))
        wl = []
        for s, v in waits:
            if v <= 0:
                continue
            assert v <= self.sem_n[s], f"wait {s}>={v} > emitted {self.sem_n[s]} ({tag})"
            if self.last_wait[eng].get(s, 0) < v:
                wl.append((s, v))
                self.last_wait[eng][s] = v
        ret = None
        if inc is not None:
            self.sem_n[inc] += inc_by
            ret = self.sem_n[inc]
        self.streams[eng].append((fn, wl, inc, inc_by, tag))
        return ret

    def val(self, s):
        return self.sem_n[s]

    def emit(self):
        block = self.es.enter_context(self.nc.Block())
        deco = {"te": block.tensor, "act": block.scalar, "dve": block.vector,
                "pool": block.gpsimd, "sp": block.sync}
        for e in ENG:
            stream = self.streams[e]
            def body(eng, stream=stream):
                for fn, wl, inc, inc_by, tag in stream:
                    for s, v in wl:
                        eng.wait_ge(self.sem_h[s], v)
                    instr = fn(eng)
                    if inc is not None:
                        assert instr is not None, f"op {tag} returned None"
                        instr.then_inc(self.sem_h[inc], inc_by)
            deco[e](body)


def build(nc, ncores=8, debug=()):
    dbg = set(debug)
    dt = nc.dram_tensor
    x_hbm = dt("x_seq", [T, H], F32, kind="ExternalInput").ap()
    tpos = dt("tpos", [1, T], F32, kind="ExternalInput").ap()
    irow_h = dt("i_row", [1, 128], F32, kind="ExternalInput").ap()
    sgn_h = dt("sgn", [128, 1], F32, kind="ExternalInput").ap()
    ident_h = dt("ident", [128, 128], F32, kind="ExternalInput").ap()
    rperm_h = dt("rperm", [128, 128], F32, kind="ExternalInput").ap()
    e2_h = dt("e2", [65, 128], F32, kind="ExternalInput").ap()
    c_h = dt("c_vec", [8, 128], F32, kind="ExternalInput").ap()
    wmod_h = dt("wmod", [H, 6 * H], F32, kind="ExternalInput").ap()
    wq_h = dt("Wq", [H, H], F32, kind="ExternalInput").ap()
    wk_h = dt("Wk", [H, H], F32, kind="ExternalInput").ap()
    wv_h = dt("Wv", [H, H], F32, kind="ExternalInput").ap()
    wo_h = dt("Wo", [H, H], F32, kind="ExternalInput").ap()
    w1_h = dt("W1", [H, 4 * H], F32, kind="ExternalInput").ap()
    w2_h = dt("W2", [4 * H, H], F32, kind="ExternalInput").ap()
    w8_h = dt("W8", [H, H], I8, kind="ExternalInput").ap()
    vrows_h = dt("vrows", [12, H], F32, kind="ExternalInput").ap()
    b1_h = dt("b1", [32, 128], F32, kind="ExternalInput").ap()
    out_h = dt("out", [TO, H], F32, kind="ExternalOutput").ap()

    kv_kind = dict(kind="ExternalOutput") if "kv" in dbg else {}
    kscr = dt("kscr", [FC, 128, T], BF16, **kv_kind).ap()
    vscr = dt("vscr", [NCH, 128, NH * 65], BF16, **kv_kind).ap()
    attnscr = dt("attnscr", [FC, 128, TO], BF16).ap()
    ccin = dt("ccin", [1, 1], F32).ap()
    ccout = dt("ccout", [1, 1], F32).ap()

    dbg_out = {}
    def dbg_dram(name, shape, dtype=F32):
        if name in dbg:
            dbg_out[name] = dt("dbg_" + name, shape, dtype, kind="ExternalOutput").ap()
            return dbg_out[name]
        return None

    es = ExitStack()
    em = Em(nc, es)
    with es:
        def sb(name, p, f, dtp):
            return es.enter_context(nc.sbuf_tensor(name, [p, f], dtp))

        es1 = ExitStack()

        def sb1(name, p, f, dtp):
            return es1.enter_context(nc.sbuf_tensor(name, [p, f], dtp))

        identS = sb("identS", 128, 128, F32)
        identB = sb("identB", 128, 128, BF16)
        rpermS = sb("rpermS", 128, 128, F32)
        rpermB = sb("rpermB", 128, 128, BF16)
        onest = sb("onest", 128, 128, F32)
        vrep = sb("vrep", 128, 128, F32)
        e2 = sb("e2S", 65, 128, F32)
        sgn_t = sb("sgn_t", 128, 1, F32)
        irow_t = sb("irow_t", 1, 128, F32)
        onesr = sb("onesr", 1, 128, F32)
        onesc = sb("onesc", 128, 1, F32)
        cfm = sb("cfm", 128, 8, F32)
        consts = sb("consts", 1, 16, F32)
        c128 = sb("c128", 128, 8, F32)
        vfm = sb("vfm", 128, 176, F32)
        b1f = sb("b1f", 32, 128, F32)
        b1fm = sb("b1fm", 128, 32, F32)
        stats1 = sb("stats1", 128, 4 * NCHO, F32)
        mx2 = sb("mx2", 128, NCHO, F32)
        bct = sb("bct", 128, 4 * H, F32)
        invf = sb("invf", 1, 128, F32)
        rowsc = sb("rowsc", 1, H, F32)
        vfmbe = sb("vfmbe", 128, 16, BF16)
        xst = sb("xst", 128, 2 * H, F32)
        hst = sb("hst", 128, 2 * H, F32)
        lnst = sb("lnst", 128, 80, F32)
        scal = sb("scal", 1, 8, F32)

        es_qf = ExitStack()
        QF = es_qf.enter_context(nc.sbuf_tensor("QF", [128, FC * TO], BF16))

        v8st = sb1("v8st", 8, 128, F32)
        tposb = sb1("tposb", 1, 2 * TT, F32)
        c8 = sb1("c8", 8, 128, F32)
        nst = sb1("nst", 128, 4 * H, BF16)
        xnfm = sb1("xnfm", 128, 2 * FC * TT, BF16)
        WQ = sb1("WQ", 128, FC * H, BF16)
        WK = sb1("WK", 128, FC * H, BF16)
        WV = sb1("WV", 128, FC * H, BF16)
        wmodc = sb1("wmodc", 128, 2 * FC * 512, F32)
        ropes = sb1("ropes", 128, 6 * TT, BF16)
        cosb = sb1("cosb", 128, TT, BF16)
        sinb = sb1("sinb", 128, TT, BF16)
        sinst = sb1("sinst", 128, TT, F32)
        thst = sb1("thst", 128, 2 * TT, F32)
        thi = sb1("thi", 128, TT, I32)
        kbf = sb1("kbf", 128, 2 * TT, BF16)
        vsp = sb1("vsp", 128, 2 * NH * 65, BF16)

        def psum(name, p, f):
            return es.enter_context(nc.psum_tensor(name, [p, f], F32))
        psS0 = psum("psS0", 128, 1024)
        psS1 = psum("psS1", 128, 1024)
        psA2 = psum("psA2", 128, 512)
        psB2 = psum("psB2", 128, 512)
        psA = psum("psA", 128, 512)
        psB = psum("psB", 128, 512)
        psG0 = psA2[:, :]
        psG1 = psB2[:, :]

        for s in ("d_ld7", "d_x0", "d_x1", "d_tp0", "d_tp1", "d_wm0", "d_wm1",
                  "d_wq", "d_wk", "d_wv", "d_v8", "d_sh0", "d_sh1",
                  "d_kb0", "d_kb1", "d_vs0", "d_vs1", "d_out",
                  "d_ka0", "d_ka1", "d_va0", "d_va1", "d_wo", "d_w10", "d_w11",
                  "d_w1s", "d_w20", "d_w21", "d_w8", "d_cc", "d_oo0", "d_oo1", "d_af0", "d_af1", "d_ab0", "d_ab1",
                  "te", "act", "dve", "pool"):
            em.sem(s)

        for nm, dst, src in (("ident", identS[:, :], ident_h), ("rperm", rpermS[:, :], rperm_h),
                             ("e2", e2[:, :], e2_h),
                             ("sgn", sgn_t[:, :], sgn_h), ("irow", irow_t[:, :], irow_h),
                             ("c8", c8[:, :], c_h), ("b1", b1f[:, :], b1_h)):
            em.on("sp", lambda e, d=dst, s=src: e.dma_start(out=d, in_=s),
                  inc="d_ld7", inc_by=16, tag="ld_" + nm)
        LD_ALL = ("d_ld7", em.val("d_ld7"))
        ld = {k: LD_ALL for k in ("ident", "e2", "sgn", "irow", "c8", "b1", "vr")}

        em.on("dve", lambda e: e.memset(onesr[:, :], 1.0), inc="dve")
        em.on("dve", lambda e: e.memset(onesc[:, :], 1.0), inc="dve")
        em.on("dve", lambda e: e.memset(onest[:, :], 1.0), inc="dve")
        em.on("dve", lambda e: e.memset(consts[:, :], 0.0), inc="dve")
        em.on("dve", lambda e: e.memset(consts[0:1, 0:1], 10000.0), inc="dve")
        v_cm = em.on("dve", lambda e: e.memset(consts[0:1, 1:2], 1.0), inc="dve")
        em.on("dve", lambda e: e.memset(
            vsp[:, :].rearrange("p (s h d) -> p s h d", s=2, h=NH)[:, :, :, 64:65], 1.0), inc="dve")

        a_ln = em.on("act", lambda e: e.activation(consts[0:1, 0:1], consts[0:1, 0:1], AF.Ln),
                     waits=[("dve", v_cm)], inc="act", tag="ln1e4")
        a_pi = em.on("act", lambda e: e.activation(consts[0:1, 1:2], consts[0:1, 1:2], AF.Arctan),
                     inc="act", tag="arctan")
        em.on("dve", lambda e: e.tensor_copy(identB[:, :], identS[:, :]),
              waits=[LD_ALL], inc="dve", tag="identb")
        v_rpb = em.on("dve", lambda e: e.tensor_copy(rpermB[:, :], rpermS[:, :]),
                      waits=[LD_ALL], inc="dve", tag="rpermb")
        em.on("dve", lambda e: e.tensor_single_scalar(consts[0:1, 1:2], consts[0:1, 1:2], 4.0, ALU.mult),
              waits=[("act", a_pi)], inc="dve", ss=True)
        em.on("dve", lambda e: e.tensor_single_scalar(consts[0:1, 2:3], consts[0:1, 1:2], 2.0, ALU.mult),
              inc="dve", ss=True)
        em.on("dve", lambda e: e.reciprocal(consts[0:1, 3:4], consts[0:1, 2:3]), inc="dve", ss=True)
        em.on("dve", lambda e: e.tensor_single_scalar(consts[0:1, 4:5], consts[0:1, 1:2], 0.5, ALU.mult),
              inc="dve", ss=True)
        v_cdone = em.on("dve", lambda e: e.tensor_single_scalar(consts[0:1, 5:6], consts[0:1, 2:3], -1.0, ALU.mult),
                        inc="dve", ss=True)
        vte = em.on("te", lambda e: e.matmul(psG1[:, 0:8], onesr[:, :], consts[0:1, 0:8],
                                             start=True, stop=True),
                    waits=[("dve", v_cdone)], inc="te", tag="c128")
        v_c128 = em.on("dve", lambda e: e.tensor_copy(c128[:, :], psG1[:, 0:8]),
                       waits=[("te", vte)], inc="dve")
        C_LN, C_PI, C_2PI, C_R2PI, C_PI2, C_N2PI = range(6)

        em.on("dve", lambda e: e.tensor_single_scalar(invf[:, :], irow_t[:, :], -1.0 / 32.0, ALU.mult),
              waits=[LD_ALL], inc="dve", tag="invf_u0")
        v_u = em.on("dve", lambda e: e.tensor_scalar(invf[:, :], invf[:, :], consts[0:1, 0:1],
                                                     None, ALU.mult),
                    waits=[("act", a_ln)], inc="dve", tag="invf_u", ss=True)
        a_invf = em.on("act", lambda e: e.activation(invf[:, :], invf[:, :], AF.Exp),
                       waits=[("dve", v_u)], inc="act", tag="invf")

        wm_pref = {}
        for nt in range(2):
            col0 = (nt % 2) * FC * 512
            wm_pref[nt] = em.on("pool", lambda e, nt=nt, col0=col0: e.dma_start(
                out=wmodc[:, col0:col0 + FC * 512].rearrange("p (c t) -> p c t", c=FC),
                in_=wmod_h[:, nt * 512:(nt + 1) * 512].rearrange("(c p) t -> p c t", p=128)),
                inc=("d_wm0" if nt % 2 == 0 else "d_wm1"), inc_by=16, tag=f"ldwm{nt}")
        w_done = {}
        for (name, Wt, src, sem) in (("wq", WQ, wq_h, "d_wq"), ("wk", WK, wk_h, "d_wk"),
                                     ("wv", WV, wv_h, "d_wv")):
            for kc in range(FC):
                em.on("pool", lambda e, Wt=Wt, src=src, kc=kc: e.dma_start(
                    out=Wt[:, kc * H:(kc + 1) * H], in_=src[kc * 128:(kc + 1) * 128, :]),
                    inc=sem, inc_by=16, tag=f"ldw_{name}{kc}")
            for kc in range(FC):
                w_done[(name, kc)] = (sem, em.val(sem))

        (V_BQ, V_BK, V_BV, V_BO, V_G1, V_BE1, V_G2, V_BE2, V_B2, V_FB, V_WS,
         V_BQP, V_BKP, V_BVP, V_SA, V_CA1, V_GA, V_SB, V_CB1, V_GB, V_TMP) = range(21)

        def C(i, n=8):
            return vfm[:, 8 * i:8 * i + n]

        v8_state = {"free": None}
        psG0_state = {"last_read": None}

        def load_vec_fm(ib, dstc, tag):
            w = [v8_state["free"]] if v8_state["free"] else []
            vl = em.on("sp", lambda e, ib=ib: e.dma_start(
                out=v8st[:, :], in_=vrows_h[ib:ib + 1, :].rearrange("o (c p) -> (o c) p", p=128)),
                waits=w, inc="d_v8", inc_by=16, tag=f"ldv_{tag}")
            tw = [("d_v8", vl), LD_ALL]
            if psG0_state["last_read"]:
                tw.append(psG0_state["last_read"])
            vt = em.on("te", lambda e: e.transpose(psG0[:, 0:8], v8st[:, :], identS[0:8, 0:8]),
                       waits=tw, inc="te", tag=f"trv_{tag}")
            v8_state["free"] = ("te", vt)
            vd = em.on("dve", lambda e, dstc=dstc: e.tensor_copy(C(dstc), psG0[:, 0:8]),
                       waits=[("te", vt)], inc="dve", tag=f"trve_{tag}")
            psG0_state["last_read"] = ("dve", vd)
            return vd

        vload = {}
        for ib, dstc, tag in ((0, V_BQ, "bq"), (1, V_BK, "bk"), (2, V_BV, "bv"),
                              (3, V_BO, "bo"), (4, V_G1, "g1"), (5, V_BE1, "be1"),
                              (6, V_G2, "g2"), (7, V_BE2, "be2"), (8, V_B2, "b2"),
                              (9, V_FB, "fb"), (10, V_WS, "ws")):
            vload[dstc] = load_vec_fm(ib, dstc, tag)
        v_be1b = em.on("dve", lambda e: e.tensor_copy(vfmbe[:, 0:8], C(V_BE1)),
                       waits=[("dve", vload[V_BE1])], inc="dve", tag="be1b")
        v_be2b = em.on("dve", lambda e: e.tensor_copy(vfmbe[:, 8:16], C(V_BE2)),
                       waits=[("dve", vload[V_BE2])], inc="dve", tag="be2b")

        def row_to_fm(src_row_ap, dstc, waits, tag):
            tw = list(waits)
            if psG0_state["last_read"]:
                tw.append(psG0_state["last_read"])
            vt = None
            for j in range(FC):
                vt = em.on("te", lambda e, j=j, src_row_ap=src_row_ap: e.transpose(
                    psG0[:, j:j + 1], src_row_ap[:, 128 * j:128 * (j + 1)], onesr[0:1, 0:1]),
                    waits=tw, inc="te", tag=f"rtf_{tag}{j}")
            vd = em.on("dve", lambda e, dstc=dstc: e.tensor_copy(C(dstc), psG0[:, 0:8]),
                       waits=[("te", vt)], inc="dve", tag=f"rtfe_{tag}")
            psG0_state["last_read"] = ("dve", vd)
            return vd

        tw = [LD_ALL]
        if psG0_state["last_read"]:
            tw.append(psG0_state["last_read"])
        vte = em.on("te", lambda e: e.transpose(psG0[:, 8:16], c8[:, :], identS[0:8, 0:8]),
                    waits=tw, inc="te", tag="ctr")
        a_silu = em.on("act", lambda e: e.activation(cfm[:, :], psG0[:, 8:16], AF.Silu),
                       waits=[("te", vte)], inc="act", tag="silu_c")
        psG0_state["last_read"] = ("act", a_silu)

        wm_free = [None, None]
        mod_fm = {}
        rowsc_state = {"free": None}
        last_evac = None
        for nt in range(12):
            slot = nt % 2
            col0 = slot * FC * 512
            waits = [wm_free[slot]] if wm_free[slot] else []
            wmsem = "d_wm0" if slot == 0 else "d_wm1"
            if nt < 2:
                vw = wm_pref[nt]
            else:
                vw = em.on("pool", lambda e, nt=nt, col0=col0: e.dma_start(
                    out=wmodc[:, col0:col0 + FC * 512].rearrange("p (c t) -> p c t", c=FC),
                    in_=wmod_h[:, nt * 512:(nt + 1) * 512].rearrange("(c p) t -> p c t", p=128)),
                    waits=waits, inc=wmsem, inc_by=16, tag=f"ldwm{nt}")
            for kc in range(FC):
                w = [(wmsem, vw), ("act", a_silu), ("dve", v_c128)]
                if last_evac is not None:
                    w.append(("dve", last_evac))
                vte = em.on("te", lambda e, kc=kc, col0=col0: e.matmul(
                    psG1[0:1, :], cfm[:, kc:kc + 1], wmodc[:, col0 + kc * 512: col0 + (kc + 1) * 512],
                    start=(kc == 0), stop=(kc == FC - 1)),
                    waits=w, inc="te", tag=f"mod{nt}_{kc}")
            wm_free[slot] = ("te", vte)
            half = nt % 2
            rw = [("te", vte)]
            if rowsc_state["free"]:
                rw.append(rowsc_state["free"])
            last_evac = em.on("dve", lambda e, half=half: e.tensor_copy(
                rowsc[:, 512 * half:512 * (half + 1)], psG1[0:1, :]),
                waits=rw, inc="dve", tag=f"mode{nt}")
            if half == 1:
                dstc = [V_SA, V_CA1, V_GA, V_SB, V_CB1, V_GB][nt // 2]
                mod_fm[dstc] = row_to_fm(rowsc[0:1, :], dstc, [("dve", last_evac)],
                                         f"mod{nt // 2}")
                rowsc_state["free"] = ("te", em.val("te"))
        v_mod = em.val("dve")

        v_ca1 = em.on("dve", lambda e: e.tensor_single_scalar(C(V_CA1), C(V_CA1), 1.0, ALU.add),
                      waits=[("dve", v_mod)], inc="dve", ss=True)
        v_cb1 = em.on("dve", lambda e: e.tensor_single_scalar(C(V_CB1), C(V_CB1), 1.0, ALU.add),
                      inc="dve", ss=True)
        v_g = {}
        for rg in (V_GA, V_GB):
            em.on("dve", lambda e, rg=rg: e.tensor_scalar(C(V_TMP), C(rg), 0.0, 2.0, ALU.is_ge, ALU.mult),
                  inc="dve", ss=True)
            em.on("dve", lambda e: e.tensor_single_scalar(C(V_TMP), C(V_TMP), -1.0, ALU.add), inc="dve", ss=True)
            em.on("dve", lambda e, rg=rg: e.scalar_tensor_tensor(C(rg), C(rg), -1.0, C(rg), ALU.mult, ALU.max),
                  inc="dve", ss=True)
            em.on("dve", lambda e, rg=rg: e.tensor_single_scalar(C(rg), C(rg), 0.25, ALU.max), inc="dve", ss=True)
            v_g[rg] = em.on("dve", lambda e, rg=rg: e.tensor_tensor(C(rg), C(rg), C(V_TMP), ALU.mult),
                            inc="dve", ss=True)
        bias_fm_done = {}
        psG1_reader = last_evac
        for wname, Wt, bsrc, bdst in (("q", WQ, V_BQ, V_BQP), ("k", WK, V_BK, V_BKP),
                                      ("v", WV, V_BV, V_BVP)):
            rw0 = None
            for j in range(2):
                for kc in range(FC):
                    vte = em.on("te", lambda e, Wt=Wt, j=j, kc=kc: e.matmul(
                        psG1[0:1, :], vfmbe[:, kc:kc + 1],
                        Wt[:, kc * H + 512 * j: kc * H + 512 * (j + 1)],
                        start=(kc == 0), stop=(kc == FC - 1)),
                        waits=[("dve", v_be1b), w_done[("w" + wname, kc)],
                               ("dve", psG1_reader)],
                        inc="te", tag=f"beW{wname}{j}_{kc}")
                rw = [("te", vte)]
                if rowsc_state["free"]:
                    rw.append(rowsc_state["free"])
                rw0 = em.on("dve", lambda e, j=j: e.tensor_copy(
                    rowsc[:, 512 * j:512 * (j + 1)], psG1[0:1, :]),
                    waits=rw, inc="dve", tag=f"beWe{wname}{j}")
                psG1_reader = rw0
            vfm_t = row_to_fm(rowsc[0:1, :], V_TMP, [("dve", rw0)], "beW" + wname)
            rowsc_state["free"] = ("te", em.val("te"))
            bias_fm_done[wname] = em.on("dve", lambda e, bsrc=bsrc, bdst=bdst: e.tensor_tensor(
                C(bdst), C(V_TMP), C(bsrc), ALU.add),
                waits=[("dve", vfm_t), ("dve", vload[bsrc])], inc="dve", tag=f"bp_{wname}", ss=True)

        wsc = {}
        for (name, Wt) in (("wq", WQ), ("wk", WK), ("wv", WV)):
            for kc in range(FC):
                wsc[(name, kc)] = em.on("dve", lambda e, Wt=Wt, kc=kc: e.tensor_scalar(
                    Wt[:, kc * H:(kc + 1) * H], Wt[:, kc * H:(kc + 1) * H],
                    vfm[:, 8 * V_G1 + kc:8 * V_G1 + kc + 1], None, ALU.mult),
                    waits=[("dve", vload[V_G1]), ("dve", bias_fm_done["v"]),
                           ("dve", bias_fm_done["q"]), ("dve", bias_fm_done["k"])],
                    inc="dve", tag=f"wsc_{name}{kc}")

        d_vfm = dbg_dram("vfm", [128, 168])
        if d_vfm is not None:
            em.on("sp", lambda e: e.dma_start(out=d_vfm, in_=vfm[:, 0:168]),
                  waits=[("dve", v_g[V_GB]), ("dve", bias_fm_done["v"])], inc="d_out", inc_by=16)

        vrep_state = {"free": None}

        def bcast_fm(colbase, slot, waits, tag):
            vd = None
            for kc in range(FC):
                w1 = list(waits)
                if vrep_state["free"]:
                    w1.append(vrep_state["free"])
                vr_ = em.on("dve", lambda e, colbase=colbase, kc=kc: e.tensor_scalar(
                    vrep[:, :], onest[:, :], vfm[:, 8 * colbase + kc:8 * colbase + kc + 1],
                    None, ALU.mult),
                    waits=w1, inc="dve", tag=f"bcr_{tag}{kc}")
                w2 = [("dve", vr_)]
                if kc >= 4 and vd is not None:
                    w2.append(("dve", vd))
                vt = em.on("te", lambda e, kc=kc: e.matmul(
                    psG1[:, 128 * (kc % 4):128 * (kc % 4) + 128], vrep[:, :], identS[:, :],
                    start=True, stop=True),
                    waits=w2, inc="te", tag=f"bcm_{tag}{kc}")
                vrep_state["free"] = ("te", vt)
                if kc % 4 == 3:
                    vd = em.on("dve", lambda e, slot=slot, kc=kc: e.tensor_copy(
                        bct[:, slot * H + 512 * (kc // 4): slot * H + 512 * (kc // 4) + 512],
                        psG1),
                        waits=[("te", vt)], inc="dve", tag=f"bce_{tag}{kc}")
            return vd

        v_bct0 = bcast_fm(V_CA1, 0, [("dve", v_ca1), ("dve", psG1_reader)], "ca1")
        v_bct1 = bcast_fm(V_SA, 1, [("dve", v_mod)], "sa")
        v_bct2 = bcast_fm(V_BVP, 2, [("dve", bias_fm_done["v"])], "bvp")

        tbl_state = {"last_use": None}

        tpos_free = [None, None]

        def gen_tables(t0):
            tps = (t0 // TT) % 2
            tpa = tposb[:, tps * TT:(tps + 1) * TT]
            tpsem = "d_tp0" if tps == 0 else "d_tp1"
            vtp = em.on("act", lambda e, tpa=tpa, t0=t0: e.dma_start(out=tpa, in_=tpos[:, t0:t0 + TT]),
                        waits=[tpos_free[tps]] if tpos_free[tps] else [],
                        inc=tpsem, inc_by=16, tag=f"ldtp{t0}")
            w = [("act", a_invf), (tpsem, vtp), ("dve", v_c128)]
            if tbl_state["last_use"]:
                w.append(tbl_state["last_use"])
            if tbl_state.get("last_use_pool"):
                w.append(tbl_state["last_use_pool"])
            vte = em.on("te", lambda e, tpa=tpa: e.matmul(
                psG0, invf[:, :], tpa, start=True, stop=True),
                waits=w, inc="te", tag=f"theta{t0}")
            tpos_free[tps] = ("te", vte)
            em.on("dve", lambda e: e.tensor_scalar(thst[:, 0:TT], psG0,
                                                   c128[:, C_R2PI:C_R2PI + 1], None, ALU.mult),
                  waits=[("te", vte)], inc="dve", tag="th1")
            em.on("dve", lambda e: e.tensor_copy(thi[:, :], thst[:, 0:TT]), inc="dve")
            em.on("dve", lambda e: e.tensor_copy(thst[:, 0:TT], thi[:, :]), inc="dve")
            v_thr = em.on("dve", lambda e: e.scalar_tensor_tensor(thst[:, TT:2 * TT], thst[:, 0:TT],
                                                                  c128[:, C_N2PI:C_N2PI + 1], psG0,
                                                                  ALU.mult, ALU.add), inc="dve", tag="th_r")
            tbl_state["thr"] = v_thr
            em.on("dve", lambda e: e.tensor_scalar(thst[:, 0:TT], thst[:, TT:2 * TT],
                                                   PI32, -2.0 * PI32, ALU.is_gt, ALU.mult), inc="dve")
            em.on("dve", lambda e: e.tensor_tensor(thst[:, TT:2 * TT], thst[:, TT:2 * TT],
                                                   thst[:, 0:TT], ALU.add), inc="dve")
            v5 = em.on("dve", lambda e: e.tensor_scalar(thst[:, TT:2 * TT], thst[:, TT:2 * TT],
                                                        -PI32, PI32, ALU.max, ALU.min), inc="dve")
            a1 = em.on("act", lambda e: e.activation(sinst[:, :], thst[:, TT:2 * TT], AF.Sin),
                       waits=[("dve", v5)], inc="act", tag="sin")
            v6 = em.on("dve", lambda e: e.tensor_scalar(sinb[:, :], sinst[:, :], sgn_t[:, :], None, ALU.mult),
                       waits=[("act", a1), LD_ALL], inc="dve", tag="sinsgn")
            v7 = em.on("dve", lambda e: e.scalar_tensor_tensor(thst[:, 0:TT], thst[:, TT:2 * TT], -1.0,
                                                               thst[:, TT:2 * TT], ALU.mult, ALU.max),
                       inc="dve", tag="absr")
            a2 = em.on("act", lambda e: e.activation(cosb[:, :], thst[:, 0:TT], AF.Sin,
                                                     bias=c128[:, C_PI2:C_PI2 + 1], scale=-1.0),
                       waits=[("dve", v7)], inc="act", tag="cos")
            return [("act", a2), ("dve", v6)]

        d_tbl = dbg_dram("tbl", [128, 2 * TT], BF16)
        d_rdbg = dbg_dram("rdbg", [128, 2 * TT + 136])
        d_h = dbg_dram("h", [T, H])
        d_xn = dbg_dram("xnh", [T, H])
        d_qf = dbg_dram("qf", [128, FC * TO], BF16)

        x_free = [None, None]
        h_free = [None, None]
        n_free = [None] * 4
        lnst_free = [None, None]
        xn_free = [None, None]
        tr_free = [None, None]
        proj_ps = [psS0[:, 0:512], psS0[:, 512:1024], psS1[:, 0:512], psS1[:, 512:1024]]
        proj_free = [None] * 4
        rope_free = [None, None]
        rps = [psG0, psG1]
        rp_free = [None, ("dve", v_bct2)]
        rope_pend = []
        kbf_free = [None, None]
        vsp_free = [None, None]
        nrm_ps_free = None

        xn_ready = {}

        def ln_stats(src_ap, L, in_waits, tag):
            em.on("dve", lambda e: e.bn_stats(L(0, 6), src_ap[:, 0:512]),
                  waits=in_waits, inc="dve", tag=f"bns0_{tag}")
            em.on("dve", lambda e: e.bn_stats(L(6, 12), src_ap[:, 512:1024]), inc="dve", tag=f"bns1_{tag}")
            em.on("dve", lambda e: e.bn_aggr(L(12, 14), L(0, 12)), inc="dve", ss=True, tag=f"bna_{tag}")
            v_ve = em.on("dve", lambda e: e.tensor_single_scalar(L(13, 14), L(13, 14), EPS, ALU.add),
                         inc="dve", ss=True)
            a_sq = em.on("act", lambda e: e.activation(L(5, 6), L(13, 14), AF.Sqrt),
                         waits=[("dve", v_ve)], inc="act")
            em.on("dve", lambda e: e.reciprocal(L(6, 7), L(5, 6)), waits=[("act", a_sq)], inc="dve")
            v_b = em.on("dve", lambda e: e.scalar_tensor_tensor(L(7, 8), L(12, 13), -1.0, L(6, 7),
                                                                ALU.mult, ALU.mult),
                        inc="dve", ss=True, tag=f"lnb_{tag}")
            return v_b, None

        ln_an = {}

        def ln_front(ti, ch):
            g = ti * 4 + ch
            xslot = g % 2
            xa = xst[:, xslot * H:(xslot + 1) * H]
            xsem = "d_x0" if xslot == 0 else "d_x1"
            vx = em.on("act", lambda e, g=g, xa=xa: e.dma_start(out=xa, in_=x_hbm[g * 128:(g + 1) * 128, :]),
                       waits=[x_free[xslot]] if x_free[xslot] else [], inc=xsem, inc_by=16, tag=f"ldx{g}")
            hs = g % 2
            ha = hst[:, hs * H:(hs + 1) * H]
            ns = g % 4
            na = nst[:, ns * H:(ns + 1) * H]
            lp = g % 2
            L = lambda a, b: lnst[:, 40 * lp + a:40 * lp + b]
            lw = [lnst_free[lp]] if lnst_free[lp] else []
            v_b, a_u = ln_stats(xa, L, [(xsem, vx)] + lw, f"ln{g}")
            if g < NCHO:
                em.on("dve", lambda e, g=g, L=L: e.tensor_copy(stats1[:, 2 * g:2 * g + 2], L(6, 8)),
                      inc="dve", ss=True)
            a_u = em.on("act", lambda e, xa=xa, ha=ha, L=L: e.activation(ha, xa, AF.Identity, bias=L(7, 8), scale=L(6, 7)),
                        waits=[("dve", v_b)] + ([h_free[hs]] if h_free[hs] else []), inc="act", tag=f"lnu{g}")
            x_free[xslot] = ("act", a_u)
            v_h1 = em.on("dve", lambda e, ha=ha: e.tensor_tensor(ha, ha, bct[:, 0:H], ALU.mult),
                         waits=[("act", a_u), ("dve", v_bct0)], inc="dve", tag=f"h1_{g}")
            v_h2 = em.on("dve", lambda e, ha=ha: e.tensor_tensor(ha, ha, bct[:, H:2 * H], ALU.add),
                         waits=[("dve", v_bct1)], inc="dve", tag=f"h2_{g}")
            hdbg = None
            if d_h is not None:
                hdbg = em.on("sp", lambda e, ha=ha, g=g: e.dma_start(out=d_h[g * 128:(g + 1) * 128, :], in_=ha),
                             waits=[("dve", v_h2)], inc="d_out", inc_by=16)
            L2 = lambda a, b: L(20 + a, 20 + b)
            v_b2, _ = ln_stats(ha, L2, [("dve", v_h2)], f"ln2_{g}")
            a_n = em.on("act", lambda e, ha=ha, na=na, L2=L2: e.activation(na, ha, AF.Identity, bias=L2(7, 8), scale=L2(6, 7)),
                        waits=[("dve", v_b2)] + ([n_free[ns]] if n_free[ns] else []), inc="act", tag=f"xnh{g}")
            h_free[hs] = ("d_out", hdbg) if hdbg else ("act", a_n)
            lnst_free[lp] = ("act", a_n)
            ln_an[g] = (a_n, ns, na)

        def ln_back(ti, ch):
            g = ti * 4 + ch
            a_n, ns, na = ln_an[g]
            xnslot = ti % 2
            xn4 = xnfm[:, :].rearrange("p (s c t) -> p s c t", s=2, c=FC)
            last_copy = None
            for grp in range(2):
                trslot = (ch * 2 + grp) % 2
                trs = (psA if trslot == 0 else psB)
                vts = None
                for q in range(4):
                    fc = grp * 4 + q
                    tw2 = [("act", a_n)]
                    if tr_free[trslot]:
                        tw2.append(tr_free[trslot])
                    if xn_free[xnslot]:
                        tw2.append(xn_free[xnslot])
                    vts = em.on("te", lambda e, na=na, fc=fc, q=q, trs=trs: e.transpose(
                        trs[:, q * 64:(q + 1) * 64].bitcast(BF16), na[:, fc * 128:(fc + 1) * 128],
                        identB[:, :]),
                        waits=tw2, inc="te", tag=f"trx{ti}_{ch}_{fc}")
                dst = xn4[:, xnslot, grp * 4:(grp + 1) * 4, ch * 128:(ch + 1) * 128]
                last_copy = em.on("dve", lambda e, dst=dst, trs=trs: e.tensor_copy(
                    dst, trs[:, 0:256].bitcast(BF16).rearrange("p (c t) -> p c t", c=4)),
                    waits=[("te", vts)], inc="dve", tag=f"trev{ti}_{ch}_{grp}")
                tr_free[trslot] = ("dve", last_copy)
            n_free[ns] = ("te", em.val("te"))
            return last_copy

        def rope_evac(ti, outc, psi, v_psum_done, is_q, out_ap, out_free):
            rs = (ti * 8 + outc) % 2
            base = rs * 3 * TT
            ra = ropes[:, base:base + TT]
            t1b = ropes[:, base + TT:base + 2 * TT]
            t2b = ropes[:, base + 2 * TT:base + 3 * TT]
            bias_col = vfm[:, 8 * (V_BQP if is_q else V_BKP) + outc: 8 * (V_BQP if is_q else V_BKP) + outc + 1]
            w = [("te", v_psum_done), ("dve", bias_fm_done["q" if is_q else "k"])]
            if rope_free[rs]:
                w.append(rope_free[rs])
            a_raw = em.on("act", lambda e, ra=ra, psi=psi, bias_col=bias_col: e.activation(
                ra, proj_ps[psi], AF.Identity, bias=bias_col, scale=1.0),
                waits=w, inc="act", tag=f"rraw{ti}_{outc}")
            prev_out = rope_free[rs]
            tw = tbl_ready

            def flush(ti=ti, outc=outc, rs=rs, ra=ra, t1b=t1b, t2b=t2b, a_raw=a_raw,
                      prev_out=prev_out, tw=tw, is_q=is_q, out_ap=out_ap, out_free=out_free):
                rp = rps[rs]
                rw = [("act", a_raw), ("dve", v_rpb)]
                if rp_free[rs]:
                    rw.append(rp_free[rs])
                v_rmm = em.on("te", lambda e: e.matmul(rp, rpermB[:, :], ra, start=True, stop=True),
                              waits=rw, inc="te", tag=f"rmm{ti}_{outc}")
                v_t1 = em.on("pool", lambda e: e.tensor_tensor(t1b, ra, cosb[:, :], ALU.mult),
                             waits=[("act", a_raw)] + tw + ([prev_out] if prev_out else []),
                             inc="pool", tag=f"rt1_{ti}_{outc}")
                v_t2 = em.on("dve", lambda e: e.tensor_tensor(t2b, rp, sinb[:, :], ALU.mult),
                             waits=[("te", v_rmm)] + tw + ([prev_out] if prev_out else []),
                             inc="dve", tag=f"rt2_{ti}_{outc}")
                v_out = em.on("dve", lambda e: e.tensor_tensor(out_ap, t1b, t2b, ALU.add),
                              waits=[("pool", v_t1), ("dve", v_t2)] + ([out_free] if out_free else []),
                              inc="dve", tag=f"ropeo{ti}_{outc}")
                rp_free[rs] = ("dve", v_t2)
                rope_free[rs] = ("dve", v_out)
                if not is_q:
                    ks = (ti * 8 + outc) % 2
                    kbsem = "d_kb0" if ks == 0 else "d_kb1"
                    v_sp = em.on("sp", lambda e: e.dma_start(
                        out=kscr[outc, :, ti * TT:(ti + 1) * TT], in_=out_ap),
                        waits=[("dve", v_out)], inc=kbsem, inc_by=16, tag=f"kspill{ti}_{outc}")
                    kbf_free[ks] = (kbsem, v_sp)

            rope_pend.append(flush)
            return a_raw

        for ti in range(NTT):
            if ti == 0:
                tbl_state["last_use"] = ("dve", em.val("dve"))
            tbl_ready = gen_tables(ti * TT)
            rp_free[0] = ("dve", tbl_state["thr"])
            xnslot = ti % 2
            xnb = xnslot * FC * TT
            if ti == 0:
                for ch in range(4):
                    ln_front(0, ch)
                for ch in range(4):
                    xn_ready[0] = ln_back(0, ch)
            v_xn = xn_ready[ti]

            proj_list = [("k", WK, False)]
            if ti < NTTO:
                proj_list.append(("q", WQ, True))
            nproj = 8 * len(proj_list)
            pts = {3: 0, 6: 1, 10: 2, 13: 3} if nproj == 16 else {1: 0, 3: 1, 5: 2, 7: 3}
            oidx = 0
            for pname, Wt, is_q in proj_list:
                for outc in range(FC):
                    psi = (ti * 12 + outc + (8 if is_q else 0)) % 4
                    w0 = [("dve", v_xn), ("dve", wsc[("w" + pname, FC - 1)])]
                    if proj_free[psi]:
                        w0.append(proj_free[psi])
                    vte = None
                    for kc in range(FC):
                        vte = em.on("te", lambda e, Wt=Wt, kc=kc, outc=outc, psi=psi, xnb=xnb: e.matmul(
                            proj_ps[psi], Wt[:, kc * H + outc * 128: kc * H + (outc + 1) * 128],
                            xnfm[:, xnb + kc * TT: xnb + (kc + 1) * TT],
                            start=(kc == 0), stop=(kc == FC - 1)),
                            waits=w0 if kc == 0 else [], inc="te", tag=f"p{pname}{ti}_{outc}_{kc}")
                    while rope_pend:
                        rope_pend.pop(0)()
                    if is_q:
                        out_ap = QF[:, outc * TO + ti * TT: outc * TO + (ti + 1) * TT]
                        a_raw = rope_evac(ti, outc, psi, vte, True, out_ap, None)
                    else:
                        ks = (ti * 8 + outc) % 2
                        out_ap = kbf[:, ks * TT:(ks + 1) * TT]
                        a_raw = rope_evac(ti, outc, psi, vte, False, out_ap, kbf_free[ks])
                    proj_free[psi] = ("act", a_raw)
                    if ti + 1 < NTT and oidx in pts:
                        ln_front(ti + 1, pts[oidx])
                    oidx += 1

            for tc in range(4):
                g = ti * 4 + tc
                vs = g % 2
                va = vsp[:, vs * NH * 65:(vs + 1) * NH * 65]
                va3 = va.rearrange("p (h d) -> p h d", h=NH)
                vevac = None
                for nt2 in range(2):
                    psi = (ti * 12 + 8 + tc * 2 + nt2) % 4
                    w0 = [("dve", v_xn), ("dve", wsc[("wv", FC - 1)])]
                    if proj_free[psi]:
                        w0.append(proj_free[psi])
                    vte = None
                    for kc in range(FC):
                        vte = em.on("te", lambda e, kc=kc, tc=tc, nt2=nt2, psi=psi, xnb=xnb: e.matmul(
                            proj_ps[psi], xnfm[:, xnb + kc * TT + tc * 128: xnb + kc * TT + (tc + 1) * 128],
                            WV[:, kc * H + nt2 * 512: kc * H + (nt2 + 1) * 512],
                            start=(kc == 0), stop=(kc == FC - 1)),
                            waits=w0 if kc == 0 else [], inc="te", tag=f"pv{g}_{nt2}_{kc}")
                    while rope_pend:
                        rope_pend.pop(0)()
                    wv2 = [("te", vte), ("dve", v_bct2)]
                    if nt2 == 0 and vsp_free[vs]:
                        wv2.append(vsp_free[vs])
                    vevac = em.on("dve", lambda e, va3=va3, nt2=nt2, psi=psi: e.tensor_tensor(
                        va3[:, nt2 * 8:(nt2 + 1) * 8, 0:64],
                        proj_ps[psi].rearrange("p (h d) -> p h d", h=8),
                        bct[:, 2 * H + nt2 * 512: 2 * H + (nt2 + 1) * 512].rearrange("p (h d) -> p h d", h=8),
                        ALU.add),
                        waits=wv2, inc="dve", tag=f"vev{g}_{nt2}")
                    proj_free[psi] = ("dve", vevac)
                vssem = "d_vs0" if vs == 0 else "d_vs1"
                v_sp = em.on("sp", lambda e, g=g, va=va: e.dma_start(out=vscr[g, :, :], in_=va),
                             waits=[("dve", vevac)], inc=vssem, inc_by=16, tag=f"vspill{g}")
                vsp_free[vs] = (vssem, v_sp)
                if ti + 1 < NTT:
                    xn_ready[ti + 1] = ln_back(ti + 1, tc)
            xn_free[xnslot] = ("te", em.val("te"))
            tbl_state["last_use"] = ("dve", em.val("dve"))
            tbl_state["last_use_pool"] = ("pool", em.val("pool"))
            if d_tbl is not None and ti == NTT - 1:
                em.on("sp", lambda e: e.dma_start(out=d_tbl[:, 0:TT], in_=cosb[:, :]),
                      waits=tbl_ready, inc="d_out", inc_by=16)
                em.on("sp", lambda e: e.dma_start(out=d_tbl[:, TT:2 * TT], in_=sinb[:, :]),
                      waits=tbl_ready, inc="d_out", inc_by=16)
                em.on("sp", lambda e: e.dma_start(out=d_rdbg[0:128, 0:2 * TT], in_=thst[:, :]),
                      waits=tbl_ready, inc="d_out", inc_by=16)
                em.on("sp", lambda e: e.dma_start(out=d_rdbg[0:1, 2 * TT:2 * TT + 128], in_=invf[:, :]),
                      waits=tbl_ready, inc="d_out", inc_by=16)
                em.on("sp", lambda e: e.dma_start(out=d_rdbg[0:128, 2 * TT + 128:2 * TT + 136], in_=c128[:, :]),
                      waits=tbl_ready, inc="d_out", inc_by=16)

        if d_qf is not None:
            em.on("sp", lambda e: e.dma_start(out=d_qf, in_=QF[:, :]),
                  waits=[("dve", em.val("dve"))], inc="d_out", inc_by=16)

        es1.close()
        BAR1 = [("te", em.val("te")), ("act", em.val("act")), ("dve", em.val("dve"))]


        es2 = ExitStack()

        def sb2(name, p, f, dtp):
            return es2.enter_context(nc.sbuf_tensor(name, [p, f], dtp))

        Kb = sb2("Kb", 128, 2 * FC * 1024, BF16)
        Vb = sb2("Vb", 128, 2 * FC * 1040, BF16)
        Pb = sb2("Pb", 128, 5 * 1024, BF16)
        accA = sb2("accA", 65, 8 * 512, F32)
        accB = sb2("accB", 65, 8 * 512, F32)
        sS = sb2("sSn", 64, 2 * 512, F32)
        attnstA = sb2("attnstA", 64, 2 * 512, BF16)
        attnstB = sb2("attnstB", 64, 2 * 512, BF16)

        NBLK = 4
        BT = 1024
        kv_free = [None, None]
        pb_free = [None] * 5
        acc2_free = [None] * FC
        norm_st = {"rcA": None, "rcB": None}
        s_banks = [psS0, psS1]
        s_bank_free = [None, None]
        pvA_banks = [psA[:, :], psA2[:, :]]
        pvB_banks = [psB[:, :], psB2[:, :]]
        pv_free = [[None, None], [None, None]]
        af_free = [None, None]
        ev3 = {}
        pv_last = {}
        pvq = []

        def qput(f):
            pvq.append(f)
            while len(pvq) > 3:
                pvq.pop(0)()

        def norm_pair(tq, pair):
            wA = [ev3[pair][0], pv_free[0][0]]
            if norm_st["rcA"]:
                wA.append(norm_st["rcA"])
            mmA = em.on("te", lambda e, pair=pair: e.matmul(
                psA[0:64, 0:512], e2[:, 0:64], accA[0:65, pair * 512:(pair + 1) * 512],
                start=True, stop=True),
                waits=wA, inc="te", tag=f"bcSA{tq}_{pair}")
            wB = [ev3[pair][1], pv_free[0][1]]
            if norm_st["rcB"]:
                wB.append(norm_st["rcB"])
            mmB = em.on("te", lambda e, pair=pair: e.matmul(
                psB[0:64, 0:512], e2[:, 0:64], accB[0:65, pair * 512:(pair + 1) * 512],
                start=True, stop=True),
                waits=wB, inc="te", tag=f"bcSB{tq}_{pair}")
            v_rcA = em.on("dve", lambda e: e.reciprocal(sS[:, 0:512], psA[0:64, 0:512]),
                          waits=[("te", mmA)], inc="dve", tag=f"rcA{tq}_{pair}")
            v_rcB = em.on("dve", lambda e: e.reciprocal(sS[:, 512:1024], psB[0:64, 0:512]),
                          waits=[("te", mmB)], inc="dve", tag=f"rcB{tq}_{pair}")
            norm_st["rcA"] = ("dve", v_rcA)
            norm_st["rcB"] = ("dve", v_rcB)
            pv_free[0] = [("dve", v_rcA), ("dve", v_rcB)]
            asl = pair % 2
            afsem = "d_af0" if asl == 0 else "d_af1"
            aw2 = [af_free[asl]] if af_free[asl] else []
            vfA = em.on("dve", lambda e, pair=pair, asl=asl: e.tensor_tensor(
                attnstA[:, asl * 512:(asl + 1) * 512],
                accA[0:64, pair * 512:(pair + 1) * 512], sS[:, 0:512], ALU.mult),
                waits=aw2, inc="dve", tag=f"afmA{tq}_{pair}")
            vfB = em.on("dve", lambda e, pair=pair, asl=asl: e.tensor_tensor(
                attnstB[:, asl * 512:(asl + 1) * 512],
                accB[0:64, pair * 512:(pair + 1) * 512], sS[:, 512:1024], ALU.mult),
                inc="dve", tag=f"afmB{tq}_{pair}")
            em.on("sp", lambda e, pair=pair, tq=tq, asl=asl: e.dma_start(
                out=attnscr[pair, 0:64, tq * TT:(tq + 1) * TT], in_=attnstA[:, asl * 512:(asl + 1) * 512]),
                waits=[("dve", vfA)], inc=afsem, inc_by=16, tag=f"afsA{tq}_{pair}")
            v_af = em.on("sp", lambda e, pair=pair, tq=tq, asl=asl: e.dma_start(
                out=attnscr[pair, 64:128, tq * TT:(tq + 1) * TT], in_=attnstB[:, asl * 512:(asl + 1) * 512]),
                waits=[("dve", vfB)], inc=afsem, inc_by=16, tag=f"afsB{tq}_{pair}")
            af_free[asl] = (afsem, v_af)
            acc2_free[pair] = ("dve", vfB)

        for tq in range(NTTO):
            for blk in range(NBLK):
                slot = blk % 2
                kasem = "d_ka0" if slot == 0 else "d_ka1"
                vasem = "d_va0" if slot == 0 else "d_va1"
                kw = [kv_free[slot]] if kv_free[slot] else []
                vka = em.on("act", lambda e, slot=slot, blk=blk: e.dma_start(
                    out=Kb[:, slot * FC * 1024:(slot + 1) * FC * 1024].rearrange("p (c t) -> p c t", c=FC),
                    in_=kscr[:, :, blk * BT:(blk + 1) * BT].rearrange("c p t -> p c t")),
                    waits=kw + ([("d_kb0", em.val("d_kb0")), ("d_kb1", em.val("d_kb1"))] + BAR1 if blk + tq == 0 else []),
                    inc=kasem, inc_by=16, tag=f"ldK{tq}_{blk}")
                vva = em.on("act", lambda e, slot=slot, blk=blk: e.dma_start(
                    out=Vb[:, slot * FC * 1040:(slot + 1) * FC * 1040].rearrange("p (c t) -> p c t", c=FC),
                    in_=vscr[blk * FC:(blk + 1) * FC, :, :].rearrange("c p t -> p c t")),
                    waits=kw + ([("d_vs0", em.val("d_vs0")), ("d_vs1", em.val("d_vs1"))] if blk + tq == 0 else []),
                    inc=vasem, inc_by=16, tag=f"ldV{tq}_{blk}")
                for pair in range(FC):
                    pvp = pair % 2
                    psA_ = pvA_banks[pvp]
                    psB_ = pvB_banks[pvp]

                    def mk_pv(c, vexp, pair=pair, pvp=pvp, psA_=psA_, psB_=psB_,
                              slot=slot, vva=vva, vasem=vasem, tq=tq, blk=blk):
                        def go():
                            psl = (pair * FC + c) % 5
                            hA, hB = 2 * pair, 2 * pair + 1
                            pvw = [("act", vexp), (vasem, vva)]
                            if c == 0 and pv_free[pvp][0]:
                                pvw += [pv_free[pvp][0], pv_free[pvp][1]]
                            em.on("te", lambda e: e.matmul(
                                psA_[0:65, :], Vb[:, slot * FC * 1040 + c * 1040 + 65 * hA: slot * FC * 1040 + c * 1040 + 65 * hA + 65],
                                Pb[:, psl * 1024: psl * 1024 + 512],
                                start=(c == 0), stop=(c == FC - 1)),
                                waits=pvw, inc="te", tag=f"pvA{tq}_{blk}_{pair}_{c}")
                            v = em.on("te", lambda e: e.matmul(
                                psB_[0:65, :], Vb[:, slot * FC * 1040 + c * 1040 + 65 * hB: slot * FC * 1040 + c * 1040 + 65 * hB + 65],
                                Pb[:, psl * 1024 + 512: psl * 1024 + 1024],
                                start=(c == 0), stop=(c == FC - 1)),
                                inc="te", tag=f"pvB{tq}_{blk}_{pair}_{c}")
                            pb_free[psl] = ("te", v)
                            pv_last[pair] = v
                        return go

                    def mk_evac(pair=pair, pvp=pvp, psA_=psA_, psB_=psB_,
                                slot=slot, tq=tq, blk=blk):
                        def go():
                            aw = [("te", pv_last[pair])]
                            if blk == 0 and acc2_free[pair]:
                                aw.append(acc2_free[pair])
                            if blk == 0:
                                v1 = em.on("dve", lambda e: e.tensor_copy(
                                    accA[0:65, pair * 512:(pair + 1) * 512], psA_[0:65, :]),
                                    waits=aw, inc="dve")
                                v2 = em.on("dve", lambda e: e.tensor_copy(
                                    accB[0:65, pair * 512:(pair + 1) * 512], psB_[0:65, :]), inc="dve")
                            else:
                                v1 = em.on("dve", lambda e: e.tensor_tensor(
                                    accA[0:65, pair * 512:(pair + 1) * 512], psA_[0:65, :],
                                    accA[0:65, pair * 512:(pair + 1) * 512], ALU.add), waits=aw, inc="dve")
                                v2 = em.on("dve", lambda e: e.tensor_tensor(
                                    accB[0:65, pair * 512:(pair + 1) * 512], psB_[0:65, :],
                                    accB[0:65, pair * 512:(pair + 1) * 512], ALU.add), inc="dve")
                            pv_free[pvp] = [("dve", v1), ("dve", v2)]
                            if blk == NBLK - 1:
                                ev3[pair] = (("dve", v1), ("dve", v2))
                            if pair == FC - 1:
                                kv_free[slot] = ("te", em.val("te"))
                        return go

                    for c in range(FC):
                        sbk = (pair * FC + c) % 2
                        sps = s_banks[sbk]
                        qsl = QF[:, pair * TO + tq * TT: pair * TO + (tq + 1) * TT]
                        kcol = slot * FC * 1024 + pair * 1024 + c * 128
                        sw = [(kasem, vka)]
                        if s_bank_free[sbk]:
                            sw.append(s_bank_free[sbk])
                        em.on("te", lambda e, sps=sps, kcol=kcol, qsl=qsl: e.matmul(
                            sps[:, 0:512], Kb[0:64, kcol:kcol + 128], qsl[0:64, :],
                            start=True, stop=True, tile_position=(0, 0)),
                            waits=sw, inc="te", tag=f"sA{tq}_{blk}_{pair}_{c}")
                        vmmS = em.on("te", lambda e, sps=sps, kcol=kcol, qsl=qsl: e.matmul(
                            sps[:, 512:1024], Kb[64:128, kcol:kcol + 128], qsl[64:128, :],
                            start=True, stop=True, tile_position=(64, 0)),
                            inc="te", tag=f"sB{tq}_{blk}_{pair}_{c}")
                        psl = (pair * FC + c) % 5
                        pw = [("te", vmmS)]
                        if pb_free[psl]:
                            pw.append(pb_free[psl])
                        va_exp = em.on("act", lambda e, sps=sps, psl=psl: e.activation(
                            Pb[:, psl * 1024:(psl + 1) * 1024], sps[:, :], AF.Exp, scale=0.125),
                            waits=pw, inc="act", tag=f"exp{tq}_{blk}_{pair}_{c}")
                        s_bank_free[sbk] = ("act", va_exp)
                        qput(mk_pv(c, va_exp))
                    qput(mk_evac())
            while pvq:
                pvq.pop(0)()
            for pair in range(FC):
                norm_pair(tq, pair)


        es2.close()
        es_qf.close()
        BAR2 = [("te", em.val("te")), ("act", em.val("act")), ("dve", em.val("dve"))]
        resid = sb("resid", 128, NCHO * H, F32)
        W8T = sb("W8T", 128, FC * H, BF16)
        mx = sb("mxs", 128, 2, F32)
        tmpR = sb("tmpR", 128, 2 * H, F32)

        es3 = ExitStack()

        def sb3(name, p, f, dtp):
            return es3.enter_context(nc.sbuf_tensor(name, [p, f], dtp))

        WO = sb3("WO", 128, FC * H, BF16)
        afb = sb3("afb", 128, 2 * FC * TT, BF16)
        w8b = sb3("w8b", 128, 2 * H, BF16)
        for kc in range(FC):
            em.on("pool", lambda e, kc=kc: e.dma_start(
                out=WO[:, kc * H:(kc + 1) * H], in_=wo_h[kc * 128:(kc + 1) * 128, :]),
                waits=BAR2 if kc == 0 else [], inc="d_wo", inc_by=16, tag=f"ldwo{kc}")
        WO_ALL = ("d_wo", em.val("d_wo"))
        v_bct2b = bcast_fm(V_BO, 2, [("dve", vload[V_BO]), ("dve", em.val("dve"))], "bo")

        wo_ps_free = [pv_free[0][0] or ("dve", 0), pv_free[1][0] or ("dve", 0)]
        y_done = {}
        afb_free = [None, None]
        afb_rdy = {}
        for g in range(NCHO):
            tile, gl = g // 4, g % 4
            if gl == 0:
                asl = tile % 2
                absem = "d_ab0" if asl == 0 else "d_ab1"
                abw = [afb_free[asl]] if afb_free[asl] else []
                if tile == 0:
                    abw += [("d_af0", em.val("d_af0")), ("d_af1", em.val("d_af1"))] + BAR2
                v_ab = em.on("act", lambda e, asl=asl, tile=tile: e.dma_start(
                    out=afb[:, asl * FC * TT:(asl + 1) * FC * TT].rearrange("p (c t) -> p c t", c=FC),
                    in_=attnscr[:, :, tile * TT:(tile + 1) * TT].rearrange("c p t -> p c t")),
                    waits=abw, inc=absem, inc_by=16, tag=f"ldaf{tile}")
                afb_rdy[tile] = (absem, v_ab)
            xslot = g % 2
            xsem = "d_x0" if xslot == 0 else "d_x1"
            xa = xst[:, xslot * H:(xslot + 1) * H]
            vx = em.on("act", lambda e, g=g, xa=xa: e.dma_start(out=xa, in_=x_hbm[g * 128:(g + 1) * 128, :]),
                       waits=[x_free[xslot]] if x_free[xslot] else [], inc=xsem, inc_by=16, tag=f"ldx2_{g}")
            hs = g % 2
            ha = hst[:, hs * H:(hs + 1) * H]
            hw = [(xsem, vx), ("dve", v_bct2b)]
            if h_free[hs]:
                hw.append(h_free[hs])
            a_u = em.on("act", lambda e, xa=xa, ha=ha, g=g: e.activation(
                ha, xa, AF.Identity, bias=stats1[:, 2 * g + 1:2 * g + 2], scale=stats1[:, 2 * g:2 * g + 1]),
                waits=hw, inc="act", tag=f"hrc{g}")
            x_free[xslot] = ("act", a_u)
            em.on("dve", lambda e, ha=ha: e.tensor_tensor(ha, ha, bct[:, 0:H], ALU.mult),
                  waits=[("act", a_u)], inc="dve", tag=f"hrc2_{g}")
            v_h = em.on("dve", lambda e, ha=ha: e.tensor_tensor(ha, ha, bct[:, H:2 * H], ALU.add),
                        inc="dve", tag=f"hrc3_{g}")
            for nt in range(2):
                psl = (g * 2 + nt) % 2
                pst = (psA if psl == 0 else psB)
                mw = [WO_ALL, ("dve", em.val("dve") if False else 0)]
                mw = [WO_ALL]
                if wo_ps_free[psl]:
                    mw.append(wo_ps_free[psl])
                mw.append(afb_rdy[tile])
                for kc in range(FC):
                    vmm = em.on("te", lambda e, gl=gl, nt=nt, kc=kc, pst=pst, asl=asl: e.matmul(
                        pst[:, :], afb[:, asl * FC * TT + kc * TT + gl * 128: asl * FC * TT + kc * TT + (gl + 1) * 128],
                        WO[:, kc * H + nt * 512: kc * H + (nt + 1) * 512],
                        start=(kc == 0), stop=(kc == FC - 1)),
                        waits=mw if kc == 0 else [], inc="te", tag=f"wo{g}_{nt}_{kc}")
                v_e1 = em.on("dve", lambda e, g=g, nt=nt, pst=pst: e.tensor_tensor(
                    resid[:, g * H + nt * 512: g * H + (nt + 1) * 512], pst[:, :],
                    bct[:, 2 * H + nt * 512: 2 * H + (nt + 1) * 512], ALU.add),
                    waits=[("te", vmm), ("dve", v_h)], inc="dve", tag=f"woe{g}_{nt}")
                wo_ps_free[psl] = ("dve", v_e1)
            v_y = em.on("dve", lambda e, g=g, ha=ha: e.tensor_tensor(
                resid[:, g * H:(g + 1) * H], resid[:, g * H:(g + 1) * H], ha, ALU.add),
                inc="dve", tag=f"y{g}")
            h_free[hs] = ("dve", v_y)
            y_done[g] = v_y
            if gl == 3:
                afb_free[asl] = ("te", em.val("te"))

        w8ps = [psS0[:, 0:512], psS0[:, 512:1024], psS1[:, 0:512], psS1[:, 512:1024]]
        w8ps_free = [None, None, None, None]
        w8_tr_done = [None, None]
        for nc2 in range(FC):
            slot = nc2 % 2
            ww = [w8_tr_done[slot]] if w8_tr_done[slot] else []
            vw = em.on("pool", lambda e, nc2=nc2, slot=slot: e.dma_start(
                out=w8b[:, slot * H:(slot + 1) * H], in_=w8_h[nc2 * 128:(nc2 + 1) * 128, :]),
                waits=ww + (BAR2 if nc2 < 2 else []), inc="d_w8", inc_by=16, tag=f"ldw8_{nc2}")
            for grp in range(2):
                bi8 = (nc2 * 2 + grp) % 4
                trs = w8ps[bi8]
                vts = None
                for qq in range(4):
                    kc = grp * 4 + qq
                    t2w = [("d_w8", vw)]
                    if w8ps_free[bi8]:
                        t2w.append(w8ps_free[bi8])
                    vts = em.on("te", lambda e, kc=kc, qq=qq, trs=trs, slot=slot: e.transpose(
                        trs[:, qq * 64:(qq + 1) * 64].bitcast(BF16),
                        w8b[:, slot * H + kc * 128: slot * H + (kc + 1) * 128],
                        identB[:, :]),
                        waits=t2w, inc="te", tag=f"w8tr{nc2}_{kc}")
                dst = W8T[:, :].rearrange("p (c n) -> p c n", c=FC)[:, grp * 4:(grp + 1) * 4, nc2 * 128:(nc2 + 1) * 128]
                vd = em.on("dve", lambda e, dst=dst, trs=trs: e.tensor_copy(
                    dst, trs[:, 0:256].bitcast(BF16).rearrange("p (c t) -> p c t", c=4)),
                    waits=[("te", vts)], inc="dve", tag=f"w8tre{nc2}_{grp}")
                w8ps_free[bi8] = ("dve", vd)
            w8_tr_done[slot] = ("te", em.val("te"))
        W8_ALL = ("dve", em.val("dve"))
        w8ps_last = ("dve", em.val("dve"))

        d_y = dbg_dram("y", [TO, H])
        if d_y is not None:
            for g in range(NCHO):
                em.on("sp", lambda e, g=g: e.dma_start(out=d_y[g * 128:(g + 1) * 128, :],
                                                       in_=resid[:, g * H:(g + 1) * H]),
                      waits=[("dve", y_done[g])], inc="d_out", inc_by=16)

        es3.close()

        es4 = ExitStack()

        def sb4(name, p, f, dtp):
            return es4.enter_context(nc.sbuf_tensor(name, [p, f], dtp))

        xn2fm = sb4("xn2fm", 128, 2 * FC * TT, BF16)
        w1cs = sb4("w1cs", 128, 2 * FC * 512, BF16)
        h1t = sb4("h1t", 128, 32 * 512, BF16)
        w2s = sb4("w2s", 128, 2 * H, BF16)
        nst2 = sb4("nst2", 128, 4 * H, BF16)
        b1pfm = sb4("b1pfm", 128, 32, F32)
        w1scr = dt("w1scr", [FC, 128, 4 * H], BF16).ap()

        BAR3 = [("te", em.val("te")), ("act", em.val("act")), ("dve", em.val("dve"))]
        vte = em.on("te", lambda e: e.transpose(psG0[:, 0:32], b1f[:, :], identS[0:32, 0:32]),
                    waits=BAR3, inc="te", tag="b1tr")
        v_b1fm = em.on("dve", lambda e: e.tensor_copy(b1fm[:, :], psG0[:, 0:32]),
                       waits=[("te", vte)], inc="dve", tag="b1tre")
        v_b2t = bcast_fm(V_B2, 0, [("dve", vload[V_B2]), ("dve", em.val("dve"))], "b2")
        v_gat = bcast_fm(V_GA, 1, [("dve", v_g[V_GA]), ("dve", em.val("dve"))], "ga")
        v_cb1t = bcast_fm(V_CB1, 2, [("dve", v_cb1), ("dve", em.val("dve"))], "cb1")
        v_sbt = bcast_fm(V_SB, 3, [("dve", v_mod), ("dve", em.val("dve"))], "sbv")

        ffn_ps = {"psS0a": psS0[:, 0:512], "psS0b": psS0[:, 512:1024],
                  "psS1a": psS1[:, 0:512], "psS1b": psS1[:, 512:1024],
                  "psA": psA[:, :], "psB": psB[:, :], "psG0": psG0, "psG1": psG1}
        ps8 = [psS0[:, 0:512], psS0[:, 512:1024], psS1[:, 0:512], psS1[:, 512:1024],
               psA[:, :], psB[:, :], psG0, psG1]
        ps8_free = [("dve", v_b1fm), ("dve", v_b1fm), w8ps_last, w8ps_last, None, None,
                    ("dve", v_b1fm), ("dve", v_sbt)]
        w1_slot_free = [None, None]
        w2_slot_free = [None, None]
        tr2_free = [None, None]
        x1_done = {}
        b1p_done = None
        lnst_free2 = [None, None]
        n_free2 = [None] * 4
        xn2_free = [None, None]
        xn2_ready = {}
        ln2_an = {}

        def ln2f(g):
            lp = g % 2
            L = lambda a, b: lnst[:, 40 * lp + a:40 * lp + b]
            ns = g % 4
            na = nst2[:, ns * H:(ns + 1) * H]
            lw = [lnst_free2[lp]] if lnst_free2[lp] else []
            yw = [("dve", y_done[g])]
            v_b, _ = ln_stats(resid[:, g * H:(g + 1) * H], L, yw + lw, f"l2_{g}")
            a_n = em.on("act", lambda e, g=g, na=na, L=L: e.activation(
                na, resid[:, g * H:(g + 1) * H], AF.Identity, bias=L(7, 8), scale=L(6, 7)),
                waits=[("dve", v_b)] + ([n_free2[ns]] if n_free2[ns] else []), inc="act", tag=f"xn2{g}")
            lnst_free2[lp] = ("act", a_n)
            ln2_an[g] = (a_n, ns, na)

        def ln2b(ti2, gl):
            g = ti2 * 4 + gl
            a_n, ns, na = ln2_an[g]
            xslot2 = ti2 % 2
            last_copy = None
            for grp in range(2):
                bi = 6 + ((gl * 2 + grp) % 2)
                trs = ps8[bi]
                vts = None
                for qq in range(4):
                    fc = grp * 4 + qq
                    tw2 = [("act", a_n)]
                    if ps8_free[bi]:
                        tw2.append(ps8_free[bi])
                    if xn2_free[xslot2]:
                        tw2.append(xn2_free[xslot2])
                    vts = em.on("te", lambda e, na=na, fc=fc, qq=qq, trs=trs: e.transpose(
                        trs[:, qq * 64:(qq + 1) * 64].bitcast(BF16), na[:, fc * 128:(fc + 1) * 128],
                        identB[:, :]),
                        waits=tw2, inc="te", tag=f"trx2{g}_{fc}")
                dst = xn2fm[:, xslot2 * FC * TT:(xslot2 + 1) * FC * TT].rearrange(
                    "p (c t) -> p c t", c=FC)[:, grp * 4:(grp + 1) * 4, gl * 128:(gl + 1) * 128]
                last_copy = em.on("dve", lambda e, dst=dst, trs=trs: e.tensor_copy(
                    dst, trs[:, 0:256].bitcast(BF16).rearrange("p (c t) -> p c t", c=4)),
                    waits=[("te", vts)], inc="dve", tag=f"trx2e{g}_{grp}")
                ps8_free[bi] = ("dve", last_copy)
            n_free2[ns] = ("te", em.val("te"))
            return last_copy

        paA = tmpR
        lnst_free3 = [None, None]
        paA_free = [None, None]

        def passA(g):
            lp = g % 2
            L3 = lambda a, b: lnst[:, 40 * lp + 20 + a:40 * lp + 20 + b]
            hs = g % 2
            na = paA[:, hs * H:(hs + 1) * H]
            lw = [lnst_free3[lp]] if lnst_free3[lp] else []
            v_b, _ = ln_stats(resid[:, g * H:(g + 1) * H], L3, [("dve", x1_done[g])] + lw, f"l3_{g}")
            em.on("dve", lambda e, g=g, L3=L3: e.tensor_copy(
                stats1[:, 2 * NCHO + 2 * g:2 * NCHO + 2 * g + 2], L3(6, 8)),
                inc="dve", ss=True, tag=f"st3_{g}")
            a_h2 = em.on("act", lambda e, na=na, L3=L3: e.activation(
                na, resid[:, g * H:(g + 1) * H], AF.Identity, bias=L3(7, 8), scale=L3(6, 7)),
                waits=[("dve", v_b)] + ([paA_free[hs]] if paA_free[hs] else []), inc="act", tag=f"h2a{g}")
            lnst_free3[lp] = ("act", a_h2)
            v_hb = em.on("pool", lambda e, na=na: e.tensor_tensor(na, na, bct[:, 2 * H:3 * H], ALU.mult),
                         waits=[("act", a_h2), ("dve", v_cb1t)], inc="pool", tag=f"h2b{g}")
            em.on("dve", lambda e, na=na: e.tensor_tensor(na, na, bct[:, 3 * H:4 * H], ALU.add),
                  waits=[("pool", v_hb), ("dve", v_sbt)], inc="dve", tag=f"h2c{g}")
            v_rm = em.on("dve", lambda e, na=na, g=g: e.reduce_max(
                mx2[:, g:g + 1], na, axis=AX.X, apply_absolute_value=True),
                inc="dve", tag=f"h2m{g}")
            paA_free[hs] = ("dve", v_rm)
            return v_rm

        for tile in range(NTTO):
            if tile == 0:
                for gl in range(4):
                    ln2f(gl)
                for gl in range(4):
                    xn2_ready[0] = ln2b(0, gl)
            v_xn2 = xn2_ready[tile]
            xnb2 = (tile % 2) * FC * TT
            for nt in range(FC):
                slot = nt % 2
                w1sem = "d_w10" if slot == 0 else "d_w11"
                col0 = slot * FC * 512
                ww = [w1_slot_free[slot]] if w1_slot_free[slot] else []
                if tile == 0:
                    vw = em.on("pool", lambda e, nt=nt, col0=col0: e.dma_start(
                        out=w1cs[:, col0:col0 + FC * 512].rearrange("p (c t) -> p c t", c=FC),
                        in_=w1_h[:, nt * 512:(nt + 1) * 512].rearrange("(c p) t -> p c t", p=128)),
                        waits=ww + (BAR3 if nt == 0 else []), inc=w1sem, inc_by=16, tag=f"ldw1_{nt}")
                    bw = [(w1sem, vw), ("dve", v_be2b)]
                    if ps8_free[4]:
                        bw.append(ps8_free[4])
                    for kc in range(FC):
                        vte = em.on("te", lambda e, kc=kc, col0=col0: e.matmul(
                            ps8[4][0:1, :], vfmbe[:, 8 + kc:8 + kc + 1],
                            w1cs[:, col0 + kc * 512: col0 + (kc + 1) * 512],
                            start=(kc == 0), stop=(kc == FC - 1)),
                            waits=bw if kc == 0 else [], inc="te", tag=f"bew1_{nt}_{kc}")
                    rw = [("te", vte)]
                    if rowsc_state["free"]:
                        rw.append(rowsc_state["free"])
                    v_rc = em.on("dve", lambda e: e.tensor_copy(rowsc[:, 0:512], ps8[4][0:1, :]),
                                 waits=rw, inc="dve", tag=f"bew1e{nt}")
                    ps8_free[4] = ("dve", v_rc)
                    vt2 = None
                    for j in range(4):
                        t2w = [("dve", v_rc)]
                        if ps8_free[5]:
                            t2w.append(ps8_free[5])
                        vt2 = em.on("te", lambda e, j=j: e.transpose(
                            ps8[5][:, j:j + 1], rowsc[:, 128 * j:128 * (j + 1)], onesr[0:1, 0:1]),
                            waits=t2w, inc="te", tag=f"b1ptr{nt}_{j}")
                    rowsc_state["free"] = ("te", vt2)
                    b1p_done = em.on("dve", lambda e, nt=nt: e.tensor_tensor(
                        b1pfm[:, 4 * nt:4 * (nt + 1)], ps8[5][:, 0:4], b1fm[:, 4 * nt:4 * (nt + 1)], ALU.add),
                        waits=[("te", vt2), ("dve", v_b1fm)], inc="dve", tag=f"b1p{nt}", ss=True)
                    ps8_free[5] = ("dve", b1p_done)
                    vsc = None
                    for kc in range(FC):
                        vsc = em.on("dve", lambda e, kc=kc, col0=col0: e.tensor_scalar(
                            w1cs[:, col0 + kc * 512: col0 + (kc + 1) * 512],
                            w1cs[:, col0 + kc * 512: col0 + (kc + 1) * 512],
                            vfm[:, 8 * V_G2 + kc:8 * V_G2 + kc + 1], None, ALU.mult),
                            waits=[("te", vte), ("dve", vload[V_G2])], inc="dve", tag=f"w1sc{nt}_{kc}")
                    vsp_ = em.on("sp", lambda e, nt=nt, col0=col0: e.dma_start(
                        out=w1scr[:, :, nt * 512:(nt + 1) * 512].rearrange("c p t -> p c t"),
                        in_=w1cs[:, col0:col0 + FC * 512].rearrange("p (c t) -> p c t", c=FC)),
                        waits=[("dve", vsc)], inc="d_w1s", inc_by=16, tag=f"w1spill{nt}")
                    w1_ready = ("dve", vsc)
                else:
                    vw = em.on("pool", lambda e, nt=nt, col0=col0: e.dma_start(
                        out=w1cs[:, col0:col0 + FC * 512].rearrange("p (c t) -> p c t", c=FC),
                        in_=w1scr[:, :, nt * 512:(nt + 1) * 512].rearrange("c p t -> p c t")),
                        waits=ww + ([("d_w1s", em.val("d_w1s"))] if tile == 1 and nt < 2 else []),
                        inc=w1sem, inc_by=16, tag=f"ldw1s{tile}_{nt}")
                    w1_ready = (w1sem, vw)
                for hsub in range(4):
                    hc = nt * 4 + hsub
                    bi = hc % 2
                    mw = [w1_ready, ("dve", v_xn2)]
                    if ps8_free[bi]:
                        mw.append(ps8_free[bi])
                    for kc in range(FC):
                        vmm = em.on("te", lambda e, kc=kc, col0=col0, hsub=hsub, bi=bi, xnb2=xnb2: e.matmul(
                            ps8[bi], w1cs[:, col0 + kc * 512 + hsub * 128: col0 + kc * 512 + (hsub + 1) * 128],
                            xn2fm[:, xnb2 + kc * TT: xnb2 + (kc + 1) * TT],
                            start=(kc == 0), stop=(kc == FC - 1)),
                            waits=mw if kc == 0 else [], inc="te", tag=f"w1mm{tile}_{hc}_{kc}")
                    a_g = em.on("act", lambda e, hc=hc, bi=bi: e.activation(
                        h1t[:, hc * 512:(hc + 1) * 512], ps8[bi], AF.Gelu,
                        bias=b1pfm[:, hc:hc + 1], scale=1.0),
                        waits=[("te", vmm)] + ([("dve", b1p_done)] if tile == 0 else []),
                        inc="act", tag=f"gelu{tile}_{hc}")
                    ps8_free[bi] = ("act", a_g)
                w1_slot_free[slot] = ("te", em.val("te"))
            xn2_free[tile % 2] = ("te", em.val("te"))
            h1_done = ("act", em.val("act"))
            for bi in range(8):
                pass
            w2_mm_last = {}
            for hidc in range(32):
                slot = hidc % 2
                w2sem = "d_w20" if slot == 0 else "d_w21"
                ww = [w2_slot_free[slot]] if w2_slot_free[slot] else []
                vw2 = em.on("pool", lambda e, hidc=hidc, slot=slot: e.dma_start(
                    out=w2s[:, slot * H:(slot + 1) * H], in_=w2_h[hidc * 128:(hidc + 1) * 128, :]),
                    waits=ww, inc=w2sem, inc_by=16, tag=f"ldw2_{tile}_{hidc}")
                for gl in range(4):
                    for nt2 in range(2):
                        bi = gl * 2 + nt2
                        mw = [(w2sem, vw2), h1_done]
                        if hidc == 0 and ps8_free[bi]:
                            mw.append(ps8_free[bi])
                        w2_mm_last[bi] = em.on("te", lambda e, hidc=hidc, gl=gl, nt2=nt2, slot=slot, bi=bi: e.matmul(
                            ps8[bi], h1t[:, hidc * 512 + gl * 128: hidc * 512 + (gl + 1) * 128],
                            w2s[:, slot * H + nt2 * 512: slot * H + (nt2 + 1) * 512],
                            start=(hidc == 0), stop=(hidc == 31)),
                            waits=mw if (hidc == 0 or gl + nt2 == 0) else [],
                            inc="te", tag=f"w2mm{tile}_{hidc}_{bi}")
                w2_slot_free[slot] = ("te", em.val("te"))
                if tile + 1 < NTTO and hidc in (6, 12, 18, 24):
                    ln2f((tile + 1) * 4 + (hidc - 6) // 6)
            for gl in range(4):
                g = tile * 4 + gl
                xslot = g % 2
                xsem = "d_x0" if xslot == 0 else "d_x1"
                xa = xst[:, xslot * H:(xslot + 1) * H]
                vx = em.on("act", lambda e, g=g, xa=xa: e.dma_start(out=xa, in_=x_hbm[g * 128:(g + 1) * 128, :]),
                           waits=[x_free[xslot]] if x_free[xslot] else [], inc=xsem, inc_by=16, tag=f"ldx3_{g}")
                v_x1 = None
                for nt2 in range(2):
                    bi = gl * 2 + nt2
                    rsl = resid[:, g * H + nt2 * 512: g * H + (nt2 + 1) * 512]
                    v_e1 = em.on("dve", lambda e, bi=bi, rsl=rsl: e.tensor_tensor(
                        rsl, ps8[bi], rsl, ALU.add),
                        waits=[("te", w2_mm_last[bi]), ("dve", v_b2t)], inc="dve", tag=f"w2e{g}_{nt2}_1")
                    em.on("dve", lambda e, rsl=rsl, nt2=nt2: e.tensor_tensor(
                        rsl, rsl, bct[:, 0 * H + nt2 * 512: 0 * H + (nt2 + 1) * 512], ALU.add),
                        inc="dve", tag=f"w2e{g}_{nt2}_2")
                    em.on("dve", lambda e, rsl=rsl, nt2=nt2: e.tensor_tensor(
                        rsl, rsl, bct[:, 1 * H + nt2 * 512: 1 * H + (nt2 + 1) * 512], ALU.mult),
                        waits=[("dve", v_gat)], inc="dve", tag=f"w2e{g}_{nt2}_3")
                    v_x1 = em.on("dve", lambda e, rsl=rsl, xa=xa, nt2=nt2: e.tensor_tensor(
                        rsl, rsl, xa[:, nt2 * 512:(nt2 + 1) * 512], ALU.add),
                        waits=[(xsem, vx)], inc="dve", tag=f"w2e{g}_{nt2}_4")
                    ps8_free[bi] = ("dve", v_e1)
                x_free[xslot] = ("dve", v_x1)
                x1_done[g] = v_x1
                passA(g)
            if tile + 1 < NTTO:
                for gl in range(4):
                    xn2_ready[tile + 1] = ln2b(tile + 1, gl)

        v_b1_ = em.on("dve", lambda e: e.tensor_tensor(C(V_TMP), C(V_GB), C(V_FB), ALU.mult),
                      waits=[("dve", v_g[V_GB]), ("dve", vload[V_FB]), ("dve", em.val("dve"))],
                      inc="dve", ss=True)
        v_bt = bcast_fm(V_TMP, 1, [("dve", v_b1_), ("dve", em.val("dve"))], "Bt")
        ps8_free[7] = ("dve", v_bt)

        v_mxr = em.on("dve", lambda e: e.reduce_max(mx[:, 0:1], mx2[:, :], axis=AX.X),
                      inc="dve", ss=True, tag="mxall")
        vte = em.on("te", lambda e: e.transpose(psG0[0:1, 0:128], mx[:, 0:1], identS[:, :]),
                    waits=[("dve", v_mxr)] + ([ps8_free[6]] if ps8_free[6] else []),
                    inc="te", tag="mxtr")
        v_m1 = em.on("dve", lambda e: e.reduce_max(scal[0:1, 0:1], psG0[0:1, 0:128], axis=AX.X),
                     waits=[("te", vte)], inc="dve")
        ps8_free[6] = ("dve", v_m1)
        v_ci = em.on("pool", lambda e: e.dma_start(out=ccin, in_=scal[0:1, 0:1]),
                     waits=[("dve", v_m1)], inc="d_cc", inc_by=16, tag="ccin")
        if ncores > 1:
            v_cc = em.on("pool", lambda e: e.collective_compute(
                "AllReduce", ALU.max, replica_groups=[list(range(ncores))],
                ins=[ccin], outs=[ccout]),
                waits=[("d_cc", v_ci)], inc="pool", inc_by=1, tag="cc")
            v_co = em.on("pool", lambda e: e.dma_start(out=scal[0:1, 1:2], in_=ccout),
                         waits=[("pool", v_cc)], inc="d_cc", inc_by=16, tag="ccout")
        else:
            v_co = em.on("pool", lambda e: e.dma_start(out=scal[0:1, 1:2], in_=ccin),
                         waits=[("d_cc", v_ci)], inc="d_cc", inc_by=16, tag="ccout1")
        tbr_done = {}
        for g in range(2):
            tbr_done[g] = em.on("dve", lambda e, g=g: e.tensor_tensor(
                tmpR[:, (g % 2) * H:(g % 2 + 1) * H], resid[:, g * H:(g + 1) * H],
                bct[:, 1 * H:2 * H], ALU.add),
                waits=[("dve", v_bt), ("dve", x1_done[g])], inc="dve", tag=f"tbr{g}")
        em.on("dve", lambda e: e.tensor_single_scalar(scal[0:1, 2:3], scal[0:1, 1:2], 1.0 / 127.0, ALU.mult),
              waits=[("d_cc", v_co)], inc="dve", ss=True)
        em.on("dve", lambda e: e.tensor_single_scalar(scal[0:1, 2:3], scal[0:1, 2:3], EPS, ALU.max),
              inc="dve", ss=True)
        v_ixs = em.on("dve", lambda e: e.reciprocal(scal[0:1, 3:4], scal[0:1, 2:3]), inc="dve", ss=True)
        vte = em.on("te", lambda e: e.matmul(psG1[:, 0:4], onesr[:, :], scal[0:1, 0:4],
                                             start=True, stop=True),
                    waits=[("dve", v_ixs)] + ([ps8_free[7]] if ps8_free[7] else []), inc="te", tag="xsbc")
        v_xs128 = em.on("dve", lambda e: e.tensor_copy(mx[:, 0:2], psG1[:, 2:4]),
                        waits=[("te", vte)], inc="dve")
        ps8_free[7] = ("dve", v_xs128)
        v_sb2 = em.on("dve", lambda e: e.tensor_scalar(bct[:, 3 * H:4 * H], bct[:, 3 * H:4 * H],
                                                       mx[:, 1:2], None, ALU.mult),
                      inc="dve", ss=True, tag="sbixs")
        v_sts = em.on("dve", lambda e: e.tensor_scalar(stats1[:, 2 * NCHO:4 * NCHO],
                                                       stats1[:, 2 * NCHO:4 * NCHO],
                                                       mx[:, 1:2], None, ALU.mult),
                      inc="dve", ss=True, tag="stixs")
        v_a1 = em.on("dve", lambda e: e.tensor_tensor(C(V_TMP), C(V_GB), C(V_WS), ALU.mult),
                     waits=[("dve", v_g[V_GB]), ("dve", vload[V_WS]), ("dve", v_xs128)], inc="dve", ss=True)
        v_a2 = em.on("dve", lambda e: e.tensor_scalar(C(V_TMP), C(V_TMP), mx[:, 0:1], None, ALU.mult),
                     inc="dve", ss=True)
        v_at = bcast_fm(V_TMP, 0, [("dve", v_a2), ("dve", em.val("dve"))], "At")
        ps8_free[7] = ("dve", v_at)

        d_x1 = dbg_dram("x1", [TO, H])
        if d_x1 is not None:
            for g in range(NCHO):
                em.on("sp", lambda e, g=g: e.dma_start(out=d_x1[g * 128:(g + 1) * 128, :],
                                                       in_=resid[:, g * H:(g + 1) * H]),
                      waits=[("dve", x1_done[g])], inc="d_out", inc_by=16)

        es4.close()

        es5 = ExitStack()

        def sb5(name, p, f, dtp):
            return es5.enter_context(nc.sbuf_tensor(name, [p, f], dtp))

        h2st = sb5("h2st", 128, 2 * H, F32)
        q8i = sb5("q8i", 128, H, I32)
        q8f = sb5("q8f", 128, 2 * H, F32)
        q8b = sb5("q8b", 128, 2 * H, BF16)
        q8fm = sb5("q8fm", 128, 2 * FC * 128, BF16)
        outst = sb5("outst", 128, 2 * H, F32)
        dbg_state = {}
        h2_free = [None, None]

        d_q8 = dbg_dram("q8", [128, H], BF16)
        q8_free = [None, None]
        out_free = [None, None]
        for g in range(NCHO):
            hs = g % 2
            na = h2st[:, hs * H:(hs + 1) * H]
            hw = [("dve", v_sts), ("dve", v_bt)]
            if h2_free[hs]:
                hw.append(h2_free[hs])
            a_h2 = em.on("act", lambda e, g=g, na=na: e.activation(
                na, resid[:, g * H:(g + 1) * H], AF.Identity,
                bias=stats1[:, 2 * NCHO + 2 * g + 1:2 * NCHO + 2 * g + 2],
                scale=stats1[:, 2 * NCHO + 2 * g:2 * NCHO + 2 * g + 1]),
                waits=hw, inc="act", tag=f"h2r{g}", ss=True)
            if g == 0 and "h2d" in dbg:
                dbgu2 = es5.enter_context(nc.sbuf_tensor("dbgu2", [128, H + 2], F32))
                em.on("dve", lambda e, na=na: e.tensor_copy(dbgu2[:, 0:H], na),
                      waits=[("act", a_h2)], inc="dve")
                v_du2 = em.on("dve", lambda e: e.tensor_copy(dbgu2[:, H:H + 2],
                                                             stats1[:, 2 * NCHO:2 * NCHO + 2]), inc="dve")
                dbg_state["u2"] = v_du2
            em.on("dve", lambda e, na=na: e.tensor_tensor(na, na, bct[:, 2 * H:3 * H], ALU.mult),
                  waits=[("act", a_h2)], inc="dve", tag=f"h2r2{g}")
            em.on("dve", lambda e, na=na: e.tensor_tensor(na, na, bct[:, 3 * H:4 * H], ALU.add),
                  inc="dve", tag=f"h2r3{g}")
            if g == 0 and "h2d" in dbg:
                dbgu3 = es5.enter_context(nc.sbuf_tensor("dbgu3", [128, 3 * H], F32))
                em.on("dve", lambda e, na=na: e.tensor_copy(dbgu3[:, 0:H], na), inc="dve")
                em.on("dve", lambda e: e.tensor_copy(dbgu3[:, H:2 * H], bct[:, 2 * H:3 * H]), inc="dve")
                v_du3 = em.on("dve", lambda e: e.tensor_copy(dbgu3[:, 2 * H:3 * H], bct[:, 3 * H:4 * H]), inc="dve")
                dbg_state["u3"] = v_du3
            qs = g % 2
            qa = q8f[:, qs * H:(qs + 1) * H]
            qb = q8b[:, qs * H:(qs + 1) * H]
            qw = [q8_free[qs]] if q8_free[qs] else []
            em.on("dve", lambda e, na=na, qa=qa: e.tensor_scalar(qa, na, 127.0, -128.0, ALU.min, ALU.max),
                  waits=qw, inc="dve", tag=f"q2{g}")
            if g == 0 and "h2d" in dbg:
                dbgw = es5.enter_context(nc.sbuf_tensor("dbgw", [128, 2 * H], F32))
                em.on("dve", lambda e, qa=qa: e.tensor_copy(dbgw[:, 0:H], qa), inc="dve")
                v_dbgw = em.on("dve", lambda e, na=na: e.tensor_copy(dbgw[:, H:2 * H], na), inc="dve")
                dbg_state["w"] = v_dbgw
            em.on("dve", lambda e, qa=qa: e.tensor_copy(q8i[:, :], qa), inc="dve", tag=f"q3{g}")
            v_q4 = em.on("dve", lambda e, na=na: e.tensor_copy(na, q8i[:, :]), inc="dve", tag=f"q4{g}")
            em.on("dve", lambda e, qa=qa, na=na: e.tensor_tensor(qa, qa, na, ALU.subtract),
                  inc="dve", tag=f"q5{g}")
            v_q6 = em.on("dve", lambda e, qa=qa, na=na, qb=qb: e.scalar_tensor_tensor(
                qb, qa, 0.5, na, ALU.is_ge, ALU.add),
                inc="dve", tag=f"q6{g}")
            h2_free[hs] = ("dve", v_q6)
            em.on("dve", lambda e, qa=qa: e.tensor_single_scalar(qa, qa, -0.5, ALU.is_le),
                  inc="dve", tag=f"q7{g}")
            v_q8 = em.on("dve", lambda e, qa=qa, qb=qb: e.tensor_tensor(qb, qb, qa, ALU.subtract),
                         inc="dve", tag=f"q8_{g}")
            if g == 0 and d_q8 is not None:
                dbgq8 = es5.enter_context(nc.sbuf_tensor("dbgq8", [128, H], BF16))
                v_dq = em.on("dve", lambda e, qb=qb: e.tensor_copy(dbgq8[:, :], qb),
                             waits=[("dve", v_q8)], inc="dve")
                dbg_state["q8"] = v_dq
            qfs = g % 2
            v_qfm = None
            for grp in range(2):
                bi = 6 + grp
                vts = None
                for qq in range(4):
                    fc = grp * 4 + qq
                    t2w = [("dve", v_q8)]
                    if ps8_free[bi]:
                        t2w.append(ps8_free[bi])
                    vts = em.on("te", lambda e, fc=fc, qq=qq, bi=bi, qb=qb: e.transpose(
                        ps8[bi][:, qq * 64:(qq + 1) * 64].bitcast(BF16),
                        qb[:, fc * 128:(fc + 1) * 128], identB[:, :]),
                        waits=t2w, inc="te", tag=f"q8tr{g}_{fc}")
                dst = q8fm[:, qfs * FC * 128:(qfs + 1) * FC * 128].rearrange(
                    "p (c t) -> p c t", c=FC)[:, grp * 4:(grp + 1) * 4, :]
                v_qfm = em.on("dve", lambda e, dst=dst, bi=bi: e.tensor_copy(
                    dst, ps8[bi][:, 0:256].bitcast(BF16).rearrange("p (c t) -> p c t", c=4)),
                    waits=[("te", vts)], inc="dve", tag=f"q8tre{g}_{grp}")
                ps8_free[bi] = ("dve", v_qfm)
            q8_free[qs] = ("dve", v_qfm)
            os = g % 2
            oa = outst[:, os * H:(os + 1) * H]
            ow = [out_free[os]] if out_free[os] else []
            for nt2 in range(2):
                bi = (g % 2) * 2 + nt2
                mw = [("dve", v_qfm), W8_ALL]
                if ps8_free[bi]:
                    mw.append(ps8_free[bi])
                for kc in range(FC):
                    vmm = em.on("te", lambda e, kc=kc, nt2=nt2, qfs=qfs, bi=bi: e.matmul(
                        ps8[bi], q8fm[:, qfs * FC * 128 + kc * 128: qfs * FC * 128 + (kc + 1) * 128],
                        W8T[:, kc * H + nt2 * 512: kc * H + (nt2 + 1) * 512],
                        start=(kc == 0), stop=(kc == FC - 1)),
                        waits=mw if kc == 0 else [], inc="te", tag=f"i8mm{g}_{nt2}_{kc}")
                v_e1 = em.on("dve", lambda e, bi=bi, oa=oa, nt2=nt2: e.tensor_tensor(
                    oa[:, nt2 * 512:(nt2 + 1) * 512], ps8[bi], bct[:, 0 * H + nt2 * 512: 0 * H + (nt2 + 1) * 512],
                    ALU.mult),
                    waits=[("te", vmm), ("dve", v_at)] + ow, inc="dve", tag=f"i8e{g}_{nt2}")
                ps8_free[bi] = ("dve", v_e1)
            v_out = em.on("dve", lambda e, oa=oa, g=g: e.tensor_tensor(
                oa, oa, tmpR[:, (g % 2) * H:(g % 2 + 1) * H], ALU.add),
                waits=[("dve", tbr_done[g])], inc="dve", tag=f"i8f{g}")
            if g + 2 < NCHO:
                tbr_done[g + 2] = em.on("dve", lambda e, g2=g + 2: e.tensor_tensor(
                    tmpR[:, (g2 % 2) * H:(g2 % 2 + 1) * H], resid[:, g2 * H:(g2 + 1) * H],
                    bct[:, 1 * H:2 * H], ALU.add),
                    waits=[("dve", v_out)], inc="dve", tag=f"tbr{g + 2}")
            oos = "d_oo0" if os == 0 else "d_oo1"
            v_st = em.on("sp", lambda e, g=g, oa=oa: e.dma_start(out=out_h[g * 128:(g + 1) * 128, :], in_=oa),
                         waits=[("dve", v_out)], inc=oos, inc_by=16, tag=f"st{g}")
            out_free[os] = (oos, v_st)

        d_xs = dbg_dram("xs", [1, 4])
        if d_xs is not None:
            em.on("sp", lambda e: e.dma_start(out=d_xs[0:1, 0:4], in_=scal[0:1, 0:4]),
                  waits=[("dve", em.val("dve"))], inc="d_out", inc_by=16)
        if "h2d" in dbg and dbg_state.get("u2"):
            em.on("sp", lambda e: e.dma_start(out=dbg_out.setdefault("u2B",
                  dt("dbg_u2B", [128, H + 2], F32, kind="ExternalOutput").ap()), in_=dbgu2[:, :]),
                  waits=[("dve", em.val("dve"))], inc="d_out", inc_by=16)
        if "h2d" in dbg and dbg_state.get("u3"):
            em.on("sp", lambda e: e.dma_start(out=dbg_out.setdefault("u3B",
                  dt("dbg_u3B", [128, 3 * H], F32, kind="ExternalOutput").ap()), in_=dbgu3[:, :]),
                  waits=[("dve", em.val("dve"))], inc="d_out", inc_by=16)
        if "h2d" in dbg and dbg_state.get("w"):
            em.on("sp", lambda e: e.dma_start(out=dbg_out.setdefault("wB",
                  dt("dbg_wB", [128, 2 * H], F32, kind="ExternalOutput").ap()), in_=dbgw[:, :]),
                  waits=[("dve", em.val("dve"))], inc="d_out", inc_by=16)
        if "q8" in dbg and dbg_state.get("q8"):
            em.on("sp", lambda e: e.dma_start(out=d_q8, in_=dbgq8[:, :]),
                  waits=[("dve", em.val("dve"))], inc="d_out", inc_by=16)
        if "h2d" in dbg and dbg_state.get("h2"):
            em.on("sp", lambda e: e.dma_start(out=dbg_out.setdefault("h2d",
                  dt("dbg_h2d", [128, H + 16], F32, kind="ExternalOutput").ap()),
                  in_=nc.virtual_tensors.get("x") if False else dbgh2[:, :]),
                  waits=[("dve", em.val("dve")), ("act", em.val("act"))], inc="d_out", inc_by=16)

        es5.close()
        em.on("sp", lambda e: e.nop(),
              waits=[("d_oo0", em.val("d_oo0")), ("d_oo1", em.val("d_oo1")),
                     ("d_out", em.val("d_out"))], tag="final")

        em.emit()

    return nc, dbg_out


def core_inputs(core, full):
    b, half = core // 2, core % 2
    x = np.asarray(full["L0_attn_x"])
    own = x[b, half * TO:(half + 1) * TO]
    oth = x[b, (1 - half) * TO:(2 - half) * TO]
    x_seq = np.concatenate([own, oth], axis=0).astype(np.float32)
    tp = np.concatenate([np.arange(half * TO, (half + 1) * TO),
                         np.arange((1 - half) * TO, (2 - half) * TO)]).astype(np.float32)[None, :]
    vrows = np.stack([full["L0_attn_bq"], full["L0_attn_bk"], full["L0_attn_bv"],
                      full["L0_attn_bo"], full["L0_attn_g1"], full["L0_attn_be1"],
                      full["L0_attn_g2"], full["L0_attn_be2"], full["L0_attn_b2"],
                      full["L0_ffn_bias"], full["L0_ffn_weight_scale"].reshape(-1),
                      np.zeros(H, np.float32)]).astype(np.float32)
    sgn = np.where((np.arange(128) % 64) < 32, -1.0, 1.0).astype(np.float32)[:, None]
    e2 = np.zeros((65, 128), np.float32)
    e2[64, :] = 1.0
    return {
        "x_seq": x_seq,
        "tpos": tp,
        "i_row": (np.arange(128) % 32).astype(np.float32)[None, :],
        "sgn": sgn,
        "ident": np.eye(128, dtype=np.float32),
        "rperm": np.eye(128, dtype=np.float32)[:, [p ^ 32 for p in range(128)]].copy(),
        "e2": e2,
        "c_vec": np.asarray(full["L0_c"])[b].reshape(8, 128).astype(np.float32),
        "wmod": np.asarray(full["L0_wmod"]).astype(np.float32),
        "Wq": np.asarray(full["L0_attn_Wq"]).astype(np.float32),
        "Wk": np.asarray(full["L0_attn_Wk"]).astype(np.float32),
        "Wv": np.asarray(full["L0_attn_Wv"]).astype(np.float32),
        "Wo": np.asarray(full["L0_attn_Wo"]).astype(np.float32),
        "W1": np.asarray(full["L0_attn_W1"]).astype(np.float32),
        "W2": np.asarray(full["L0_attn_W2"]).astype(np.float32),
        "W8": np.asarray(full["L0_ffn_weight_int8"]),
        "vrows": vrows,
        "b1": np.asarray(full["L0_attn_b1"]).reshape(32, 128).astype(np.float32),
    }


def kernel(**inputs):
    nc = bass.Bass("TRN2", target_bir_lowering=False, debug=False,
                   detect_race_conditions=False)
    build(nc, ncores=8)
    in_maps = [core_inputs(c, inputs) for c in range(8)]
    res = run_bass_kernel_spmd(nc, in_maps, core_ids=list(range(8)))
    out = np.zeros((4, 4096, 1024), dtype=np.float32)
    for c in range(8):
        b, half = c // 2, c % 2
        out[b, half * TO:(half + 1) * TO] = res.results[c]["out"]
    return out
